# version 05
# baseline (speedup 1.0000x reference)
import math
import numpy as np
import concourse.bass as bass
import concourse.mybir as mybir
import concourse.bass_utils as bass_utils

F32 = mybir.dt.float32
BF16 = mybir.dt.bfloat16
I32 = mybir.dt.int32
AF = mybir.ActivationFunctionType
ALU = mybir.AluOpType
AX = mybir.AxisListType.X

D = 4096
NT = 8192
OWN = 1024
H = 4
CD = 32
SC = 1.0 / math.sqrt(128.0)
SHIFT = 18.0
EPS = 1e-5
PI = math.pi
TWO_PI = 2.0 * math.pi
LN1E4 = math.log(10000.0)


class Sem:
    def __init__(self, nc, name):
        self.h = nc.alloc_semaphore(name)
        self.n = 0

    def bump(self, k=1):
        self.n += k
        return self.n


class Em:
    ENGS = ("sync", "gpsimd", "scalar", "vector", "tensor")

    def __init__(self):
        self.q = {k: [] for k in self.ENGS}

    def w(self, eng, sem, val=None):
        v = sem.n if val is None else val
        if v > 0:
            self.q[eng].append(lambda e, s=sem.h, vv=v: e.wait_ge(s, vv))

    def op(self, eng, fn, sem=None, k=1):
        if sem is not None:
            sem.bump(k)
            self.q[eng].append(lambda e, f=fn, s=sem.h, kk=k: f(e).then_inc(s, kk))
        else:
            self.q[eng].append(lambda e, f=fn: f(e))


def build_nc():
    nc = bass.Bass("TRN2", target_bir_lowering=False)

    x_my = nc.dram_tensor("x_my", [OWN, D], F32, kind="ExternalInput")
    wq_t = nc.dram_tensor("wq_t", [128, CD, 512], F32, kind="ExternalInput")
    wk_t = nc.dram_tensor("wk_t", [128, CD, 128], F32, kind="ExternalInput")
    wv_t = nc.dram_tensor("wv_t", [128, CD, 128], F32, kind="ExternalInput")
    wo_d = nc.dram_tensor("wo_d", [D, D], F32, kind="ExternalInput")
    wbT_d = nc.dram_tensor("wbT_d", [D, D], F32, kind="ExternalInput")
    a_t = nc.dram_tensor("a_t", [128, CD, 128], F32, kind="ExternalInput")
    bcat_d = nc.dram_tensor("bcat_d", [128, D], F32, kind="ExternalInput")
    g9_d = nc.dram_tensor("g9_d", [128, CD, 9], F32, kind="ExternalInput")
    bb_d = nc.dram_tensor("bb_d", [1, D], F32, kind="ExternalInput")
    bthr_d = nc.dram_tensor("bthr_d", [1, 1], F32, kind="ExternalInput")
    out_d = nc.dram_tensor("out_d", [OWN, D], F32, kind="ExternalOutput")

    xln_in0 = nc.dram_tensor("xln_in0", [512, D], BF16, kind="Internal")
    xln_in1 = nc.dram_tensor("xln_in1", [512, D], BF16, kind="Internal")
    xag0 = nc.dram_tensor("xag0", [NT // 2, D], BF16, kind="Internal")
    xag1 = nc.dram_tensor("xag1", [NT // 2, D], BF16, kind="Internal")
    o_in0 = nc.dram_tensor("o_in0", [512, NT // 2], BF16, kind="Internal")
    o_in1 = nc.dram_tensor("o_in1", [512, NT // 2], BF16, kind="Internal")
    o_all0 = nc.dram_tensor("o_all0", [D, NT // 2], BF16, kind="Internal")
    o_all1 = nc.dram_tensor("o_all1", [D, NT // 2], BF16, kind="Internal")
    x2_scr = nc.dram_tensor("x2_scr", [OWN, D], F32, kind="Internal")
    xl2_scr = nc.dram_tensor("xl2_scr", [OWN, D], BF16, kind="Internal")

    ones_col_bf = nc.alloc_sbuf_tensor("ones_col_bf", [128, 1], BF16)
    ones_row_bf = nc.alloc_sbuf_tensor("ones_row_bf", [1, 128], BF16)
    ones_row_f = nc.alloc_sbuf_tensor("ones_row_f", [1, 128], F32)
    ident_bf = nc.alloc_sbuf_tensor("ident_bf", [128, 128], BF16)
    pswap_bf = nc.alloc_sbuf_tensor("pswap_bf", [128, 128], BF16)
    e2_bf = nc.alloc_sbuf_tensor("e2_bf", [8, 128], BF16)
    i9_f = nc.alloc_sbuf_tensor("i9_f", [9, 9], F32)
    masks_bf = nc.alloc_sbuf_tensor("masks_bf", [128, 4 * 512], BF16)
    eps_c = nc.alloc_sbuf_tensor("eps_c", [128, 1], F32)
    nh_c = nc.alloc_sbuf_tensor("nh_c", [128, 1], F32)
    sc_c = nc.alloc_sbuf_tensor("sc_c", [128, 1], F32)
    nshift_c = nc.alloc_sbuf_tensor("nshift_c", [128, 1], F32)
    none_c = nc.alloc_sbuf_tensor("none_c", [128, 1], F32)
    nlt_c = nc.alloc_sbuf_tensor("nlt_c", [128, 1], F32)
    nbthr_c = nc.alloc_sbuf_tensor("nbthr_c", [128, 1], F32)
    bthr_sb = nc.alloc_sbuf_tensor("bthr_sb", [1, 1], F32)
    bb_row_bf = nc.alloc_sbuf_tensor("bb_row_bf", [1, D], BF16)
    g9_sb = nc.alloc_sbuf_tensor("g9_sb", [128, CD * 9], BF16)
    at_sb = nc.alloc_sbuf_tensor("at_sb", [128, D], BF16)
    bc_sb = nc.alloc_sbuf_tensor("bc_sb", [128, D], BF16)
    sum1 = nc.alloc_sbuf_tensor("sum1", [128, 8], F32)
    sq1 = nc.alloc_sbuf_tensor("sq1", [128, 8], F32)
    negm1 = nc.alloc_sbuf_tensor("negm1", [128, 8], F32)
    rstd1 = nc.alloc_sbuf_tensor("rstd1", [128, 8], F32)
    var1 = nc.alloc_sbuf_tensor("var1", [128, 8], F32)
    tmp1 = nc.alloc_sbuf_tensor("tmp1", [128, 8], F32)
    sum2 = nc.alloc_sbuf_tensor("sum2", [128, 8], F32)
    sq2a = nc.alloc_sbuf_tensor("sq2a", [128, 8], F32)
    smP = nc.alloc_sbuf_tensor("smP", [128, 64], F32)
    sqP = nc.alloc_sbuf_tensor("sqP", [128, 64], F32)
    gmax = nc.alloc_sbuf_tensor("gmax", [128, 1], F32)
    gate_e = nc.alloc_sbuf_tensor("gate_e", [128, 8], F32)
    gsum = nc.alloc_sbuf_tensor("gsum", [128, 1], F32)
    grec = nc.alloc_sbuf_tensor("grec", [128, 1], F32)
    esig = nc.alloc_sbuf_tensor("esig", [128, 1], F32)
    sig = nc.alloc_sbuf_tensor("sig", [128, 1], F32)
    thr_c = nc.alloc_sbuf_tensor("thr_c", [128, 1], F32)
    w8 = nc.alloc_sbuf_tensor("w8", [128, 8], F32)
    wsum = nc.alloc_sbuf_tensor("wsum", [128, 1], F32)
    wiz = nc.alloc_sbuf_tensor("wiz", [128, 1], F32)
    wrec = nc.alloc_sbuf_tensor("wrec", [128, 1], F32)
    w_bf = nc.alloc_sbuf_tensor("w_bf", [128, 8], BF16)
    rden = nc.alloc_sbuf_tensor("rden", [1, 512], F32)

    got = nc.bump_sbuf(175104)
    assert got is not None, "arena alloc failed"
    A = got[0]

    def at(name, shape, dtype, off):
        return nc.alloc_sbuf_tensor_at(name, shape, dtype, offset=A + off)

    cos_t = at("cos_t", [128, 4096], F32, 0)
    sin_t = at("sin_t", [128, 4096], F32, 16384)
    it1 = at("it1", [128, 512], F32, 32768)
    it2 = at("it2", [128, 512], I32, 34816)
    it3 = at("it3", [128, 512], F32, 36864)
    pswap_tmp = at("pswap_tmp", [128, 128], BF16, 38912)
    ifr = at("ifr", [1, 128], F32, 39168)
    c64 = at("c64", [1, 128], F32, 39680)
    invfr = at("invfr", [1, 128], F32, 40192)
    pos_r = at("pos_r", [1, 512], F32, 40704)
    x_tb = at("x_tb", [128, 2 * 4096], F32, 32768)
    xln_tb = at("xln_tb", [128, 2 * 4096], BF16, 65536)
    sq_scr = at("sq_scr", [128, 4096], BF16, 131072)
    q_sb = at("q_sb", [128, H * 4096], BF16, 32768)
    k_sb = at("k_sb", [128, 4096], BF16, 65536)
    v_sb = at("v_sb", [128, 4096], BF16, 73728)
    wq_sb = at("wq_sb", [128, CD * 512], BF16, 81920)
    wk_sb = at("wk_sb", [128, CD * 128], BF16, 114688)
    wv_sb = at("wv_sb", [128, CD * 128], BF16, 122880)
    xr_sb = at("xr_sb", [128, CD * 512], BF16, 131072)
    raw_bf = at("raw_bf", [128, 512], BF16, 163840)
    c1f = at("c1f", [128, 512], F32, 164864)
    c2f = at("c2f", [128, 512], F32, 166912)
    vt_bf = at("vt_bf", [128, 512], BF16, 168960)
    p_bf = at("p_bf", [128, 2 * 512], BF16, 169984)
    obf = at("obf", [128, 512], BF16, 172032)
    rb_sb = at("rb_sb", [128, 512], F32, 173056)
    o_sb = at("o_sb", [128, CD * 1024], BF16, 32768)
    wo_sb = at("wo_sb", [128, 2 * CD * 512], BF16, 98304)
    xres = at("xres", [128, 2 * 512], F32, 163840)
    x2st = at("x2st", [128, 2 * 512], F32, 167936)
    sq2s = at("sq2s", [128, 512], BF16, 172032)
    xl2t = at("xl2t", [128, CD * 1024], BF16, 0)
    xl2w = at("xl2w", [128, 2 * 4096], BF16, 65536)
    wb_sb = at("wb_sb", [128, 2 * CD * 512], BF16, 65536)
    x2rd = at("x2rd", [128, 2 * 4096], F32, 131072)
    h_bf = at("h_bf", [128, 1024], BF16, 131072)
    hp_bf = at("hp_bf", [128, 1024], BF16, 133120)
    wexp_sb = at("wexp_sb", [128, 1024], BF16, 135168)
    wt_sb = at("wt_sb", [8, 1024], BF16, 137216)
    gt_sb = at("gt_sb", [9, 1024], F32, 139264)
    x2c = at("x2c", [128, 2 * 512], F32, 143360)
    outst = at("outst", [128, 2 * 512], F32, 147456)

    ps = [nc.alloc_psum_tensor(f"ps{i}", [128, 512], F32) for i in range(7)]
    ps7b = nc.alloc_psum_tensor("ps7b", [128, 1024], BF16)

    em = Em()
    sL = Sem(nc, "sL")
    sG = Sem(nc, "sG")
    sT = Sem(nc, "sT")
    sA = Sem(nc, "sA")
    sV = Sem(nc, "sV")
    sB = Sem(nc, "sB")

    def g_init(g):
        g.memset(ones_col_bf[:], 1.0)
        g.memset(ones_row_bf[:], 1.0)
        g.memset(ones_row_f[:], 1.0)
        g.memset(eps_c[:], EPS)
        g.memset(nh_c[:], -0.5)
        g.memset(sc_c[:], SC)
        g.memset(nshift_c[:], -SHIFT)
        g.memset(none_c[:], -1.0)
        g.memset(nlt_c[:], -LN1E4 / 64.0)
        g.memset(ident_bf[:], 1.0)
        g.affine_select(ident_bf[:], ident_bf[:], [[1, 128]], ALU.is_equal, 0.0,
                        base=0, channel_multiplier=-1)
        g.memset(pswap_bf[:], -1.0)
        g.affine_select(pswap_bf[:], pswap_bf[:], [[1, 128]], ALU.is_equal, 0.0,
                        base=64, channel_multiplier=-1)
        g.memset(pswap_tmp[:], 1.0)
        g.affine_select(pswap_tmp[:], pswap_tmp[:], [[1, 128]], ALU.is_equal, 0.0,
                        base=-64, channel_multiplier=-1)
        g.tensor_tensor(pswap_bf[:], pswap_bf[:], pswap_tmp[:], ALU.add)
        g.memset(e2_bf[:], 2.0)
        g.affine_select(e2_bf[:], e2_bf[:], [[1, 128]], ALU.is_ge, 0.0,
                        base=0, channel_multiplier=-16)
        g.affine_select(e2_bf[:], e2_bf[:], [[-1, 128]], ALU.is_ge, 0.0,
                        base=15, channel_multiplier=16)
        g.memset(i9_f[:], 1.0)
        g.affine_select(i9_f[:], i9_f[:], [[1, 9]], ALU.is_equal, 0.0,
                        base=0, channel_multiplier=-1)
        g.memset(masks_bf[:], 1.0)
        for j in range(4):
            g.affine_select(masks_bf[:, j * 512:(j + 1) * 512],
                            masks_bf[:, j * 512:(j + 1) * 512],
                            [[1, 512]], ALU.is_ge, 0.0,
                            base=-128 * j, channel_multiplier=-1)
        g.iota(ifr[:], [[1, 128]], base=0, channel_multiplier=0,
               allow_small_or_imprecise_dtypes=True)
        g.memset(c64[:], 64.0)
        g.affine_select(c64[:], c64[:], [[1, 128]], ALU.is_ge, 0.0,
                        base=-64, channel_multiplier=0)
        return g.engine_nop()

    em.op("gpsimd", g_init, sG)
    init_done = sG.n

    def cast_load(dst_ap, src_ap):
        em.op("gpsimd", lambda g, d=dst_ap, s=src_ap: g.dma_start(out=d, in_=s),
              sG, 16)

    cast_load(wq_sb[:], wq_t[:, :, :].rearrange("p a b -> p (a b)"))
    cast_load(wk_sb[:], wk_t[:, :, :].rearrange("p a b -> p (a b)"))
    cast_load(wv_sb[:], wv_t[:, :, :].rearrange("p a b -> p (a b)"))
    cast_load(at_sb[:], a_t[:, :, :].rearrange("p a b -> p (a b)"))
    cast_load(bc_sb[:], bcat_d[:, :])
    cast_load(g9_sb[:], g9_d[:, :, :].rearrange("p a b -> p (a b)"))
    cast_load(bb_row_bf[:], bb_d[:, :])
    cast_load(bthr_sb[:], bthr_d[:, :])
    w_loaded = sG.n

    em.w("vector", sG, init_done)
    em.op("vector", lambda v: v.tensor_sub(ifr[:], ifr[:], c64[:]), sV)
    em.w("scalar", sV)
    em.op("scalar", lambda s: s.activation(out=invfr[:], in_=ifr[:], func=AF.Exp,
                                           scale=nlt_c[0:1, 0:1]), sA)
    invfr_done = sA.n

    for ch in range(8):
        if ch > 0:
            em.w("gpsimd", sT)
        em.op("gpsimd", lambda g, b=512 * ch: g.iota(
            pos_r[:], [[1, 512]], base=b, channel_multiplier=0,
            allow_small_or_imprecise_dtypes=True), sG)
        em.w("tensor", sG)
        em.w("tensor", sA, invfr_done)
        if ch > 0:
            em.w("tensor", sA)
        em.op("tensor", lambda t: t.matmul(ps[6][:, :], invfr[:], pos_r[:],
                                           start=True, stop=True), sT)
        for phase in range(2):
            add = PI / 2 if phase == 0 else 0.0
            dst = cos_t if phase == 0 else sin_t

            def vchain(v, a=add):
                v.tensor_scalar(it1[:], ps[6][:, :], a, 1.0 / TWO_PI,
                                ALU.add, ALU.mult)
                v.tensor_copy(it2[:], it1[:])
                v.tensor_copy(it3[:], it2[:])
                if a != 0.0:
                    v.tensor_scalar(it1[:], ps[6][:, :], a, None, ALU.add)
                    return v.scalar_tensor_tensor(it1[:], it3[:], -TWO_PI, it1[:],
                                                  ALU.mult, ALU.add)
                return v.scalar_tensor_tensor(it1[:], it3[:], -TWO_PI, ps[6][:, :],
                                              ALU.mult, ALU.add)

            em.w("vector", sT)
            if phase == 1:
                em.w("vector", sA)
            em.op("vector", vchain, sV)
            em.w("scalar", sV)
            em.op("scalar", lambda s, d=dst, c=ch: s.activation(
                out=d[:, c * 512:(c + 1) * 512], in_=it1[:], func=AF.Sin), sA)
    trig_done_a = sA.n
    ps6_free = sV.n

    em.w("tensor", sG, w_loaded)
    em.w("tensor", sV, ps6_free)
    em.op("tensor", lambda t: t.matmul(ps[6][:, 0:1], ones_row_f[:], bthr_sb[:],
                                       start=True, stop=True), sT)
    em.w("vector", sT)
    em.op("vector", lambda v: v.tensor_scalar(nbthr_c[:], ps[6][:, 0:1], -1.0,
                                              None, ALU.mult), sV)
    ps6_free = sV.n
    ps7_free = 0

    ph1_s = {}
    ph1_v = {}
    for tb in range(8):
        if tb == 0:
            em.w("sync", sA, trig_done_a)
        if tb >= 2:
            em.w("sync", sA, ph1_s[tb - 2])
            em.w("sync", sV, ph1_v[tb - 2])
        em.op("sync", lambda e, i=tb: e.dma_start(
            out=x_tb[:, (i % 2) * 4096:(i % 2) * 4096 + 4096],
            in_=x_my[i * 128:(i + 1) * 128, :]), sL, 16)
        em.w("scalar", sL)
        em.op("scalar", lambda s, i=tb: s.activation(
            out=sq_scr[:], in_=x_tb[:, (i % 2) * 4096:(i % 2) * 4096 + 4096],
            func=AF.Square, accum_out=sq1[:, i:i + 1]), sA)
        ph1_s[tb] = sA.n
        em.w("vector", sL)
        em.op("vector", lambda v, i=tb: v.reduce_sum(
            sum1[:, i:i + 1], x_tb[:, (i % 2) * 4096:(i % 2) * 4096 + 4096], AX), sV)
        ph1_v[tb] = sV.n

    em.w("vector", sA, ph1_s[7])
    em.w("vector", sV)

    em.op("vector", lambda v: v.tensor_scalar(negm1[:], sum1[:], -1.0 / D,
                                              None, ALU.mult), sB)
    em.w("vector", sB)
    em.op("vector", lambda v: v.tensor_scalar(var1[:], sq1[:], 1.0 / D,
                                              None, ALU.mult), sB)
    em.w("vector", sB)
    em.op("vector", lambda v: v.tensor_tensor(tmp1[:], negm1[:], negm1[:],
                                              ALU.mult), sB)
    em.w("vector", sB)
    em.op("vector", lambda v: v.tensor_tensor(var1[:], var1[:], tmp1[:],
                                              ALU.subtract), sV)
    em.w("scalar", sV)

    def s_stats1(s):
        s.activation(out=tmp1[:], in_=var1[:], func=AF.Ln, bias=eps_c[:, 0:1])
        return s.activation(out=rstd1[:], in_=tmp1[:], func=AF.Exp,
                            scale=nh_c[:, 0:1])

    em.op("scalar", s_stats1, sA)
    ln1_stats = sA.n

    ph1b_apply = {}
    ph1b_wr = {}
    ag0_done = None
    for tb in range(8):
        if tb == 0:
            em.w("sync", sA, ln1_stats)
        if tb >= 2:
            em.w("sync", sV, ph1b_apply[tb - 2])
        em.op("sync", lambda e, i=tb: e.dma_start(
            out=x_tb[:, (i % 2) * 4096:(i % 2) * 4096 + 4096],
            in_=x_my[i * 128:(i + 1) * 128, :]), sL, 16)
        em.w("vector", sL)
        em.w("vector", sA, ln1_stats)
        if tb >= 2:
            em.w("vector", sL, ph1b_wr[tb - 2])
        em.op("vector", lambda v, i=tb: v.tensor_scalar(
            xln_tb[:, (i % 2) * 4096:(i % 2) * 4096 + 4096],
            x_tb[:, (i % 2) * 4096:(i % 2) * 4096 + 4096],
            negm1[:, i:i + 1], rstd1[:, i:i + 1], ALU.add, ALU.mult), sV)
        ph1b_apply[tb] = sV.n
        em.w("sync", sV)
        dst = xln_in0 if tb < 4 else xln_in1
        em.op("sync", lambda e, i=tb, d=dst: e.dma_start(
            out=d[(i % 4) * 128:(i % 4 + 1) * 128, :],
            in_=xln_tb[:, (i % 2) * 4096:(i % 2) * 4096 + 4096]), sL, 16)
        ph1b_wr[tb] = sL.n
        if tb == 3:
            em.w("gpsimd", sL)
            em.op("gpsimd", lambda g: g.collective_compute(
                "AllGather", ALU.bypass, replica_groups=[list(range(8))],
                ins=[xln_in0.ap().opt()], outs=[xag0.ap().opt()]), sG)
            ag0_done = sG.n
    em.w("gpsimd", sL)
    em.op("gpsimd", lambda g: g.collective_compute(
        "AllGather", ALU.bypass, replica_groups=[list(range(8))],
        ins=[xln_in1.ap().opt()], outs=[xag1.ap().opt()]), sG)
    ag1_done = sG.n

    o_wr_last = 0
    qkv_done = None
    rope_fin = ps6_free
    prev_rope_v = None
    prev_vt_v = None
    wo_ld = {}
    for b in range(2):
        if b == 1:
            ph3_v_end = sV.n
        xagb = xag0 if b == 0 else xag1
        agb_done = ag0_done if b == 0 else ag1_done
        o_inb = o_in0 if b == 0 else o_in1
        for q in range(8):
            posb = q * 512
            if qkv_done is not None:
                em.w("sync", sT, qkv_done)
            em.w("sync", sG, agb_done)
            for cd in range(CD):
                em.op("sync", lambda e, x=xagb, c=q, d=cd: e.dma_start(
                    out=xr_sb[:, d * 512:(d + 1) * 512],
                    in_=x[c * 512:(c + 1) * 512, d * 128:(d + 1) * 128],
                    transpose=True), sL, 16)
            em.w("tensor", sL)
            em.w("tensor", sG, w_loaded)
            if b == 1 and q == 0:
                em.w("tensor", sV, ph3_v_end)
            if prev_rope_v is not None:
                em.w("tensor", sV, prev_rope_v)
                em.w("tensor", sV, prev_vt_v)

            def qkv(t):
                r = None
                for cd in range(CD):
                    st = cd == 0
                    sp = cd == CD - 1
                    for hh in range(H):
                        r = t.matmul(
                            ps[hh][:, :],
                            wq_sb[:, cd * 512 + hh * 128: cd * 512 + (hh + 1) * 128],
                            xr_sb[:, cd * 512:(cd + 1) * 512], start=st, stop=sp)
                    r = t.matmul(ps[4][:, :], wk_sb[:, cd * 128:(cd + 1) * 128],
                                 xr_sb[:, cd * 512:(cd + 1) * 512], start=st, stop=sp)
                    r = t.matmul(ps[5][:, :], wv_sb[:, cd * 128:(cd + 1) * 128],
                                 xr_sb[:, cd * 512:(cd + 1) * 512], start=st, stop=sp)
                return r

            em.op("tensor", qkv, sT)
            qkv_done = sT.n
            for j in range(5):
                src = ps[j] if j < 4 else ps[4]
                if j < 4:
                    dstap = q_sb[:, j * 4096 + posb: j * 4096 + posb + 512]
                else:
                    dstap = k_sb[:, posb: posb + 512]
                em.w("vector", sT, qkv_done)
                em.w("vector", sA, trig_done_a)

                def v_pre(v, s_=src, pp=posb):
                    v.tensor_copy(raw_bf[:], s_[:, :])
                    return v.tensor_tensor(c1f[:], s_[:, :],
                                           cos_t[:, pp:pp + 512], ALU.mult)

                em.op("vector", v_pre, sV)
                em.w("tensor", sV)
                em.w("tensor", sV, rope_fin)
                em.op("tensor", lambda t: t.matmul(ps[6][:, :], pswap_bf[:],
                                                   raw_bf[:], start=True,
                                                   stop=True), sT)
                em.w("vector", sT)

                def v_fin(v, d=dstap, pp=posb):
                    v.tensor_tensor(c2f[:], ps[6][:, :], sin_t[:, pp:pp + 512],
                                    ALU.mult)
                    return v.tensor_tensor(d, c1f[:], c2f[:], ALU.add)

                em.op("vector", v_fin, sV)
                rope_fin = sV.n
            prev_rope_v = rope_fin
            em.w("vector", sT, qkv_done)
            em.op("vector", lambda v: v.tensor_copy(vt_bf[:], ps[5][:, :]), sV)
            em.w("tensor", sV)
            em.w("tensor", sV, ps7_free)

            def v_tr(t):
                r = None
                for s4 in range(4):
                    r = t.transpose(ps7b[:, s4 * 128:(s4 + 1) * 128],
                                    vt_bf[:, s4 * 128:(s4 + 1) * 128], ident_bf[:])
                return r

            em.op("tensor", v_tr, sT)
            em.w("vector", sT)
            em.op("vector", lambda v, pp=posb: v.tensor_copy(
                v_sb[:, pp:pp + 512], ps7b[:, 0:512]), sV)
            ps7_free = sV.n
            prev_vt_v = sV.n

        if b == 1:
            for m in range(2):
                em.w("gpsimd", sT, qkv_done)
                em.op("gpsimd", lambda g, m_=m: g.dma_start(
                    out=wo_sb[:, (m_ % 2) * CD * 512:
                              (m_ % 2) * CD * 512 + CD * 512]
                    .rearrange("p (c j) -> p c j", c=CD),
                    in_=wo_d[:, m_ * 512:(m_ + 1) * 512]
                    .rearrange("(c p) j -> p c j", p=128)), sG, 16)
                wo_ld[m] = sG.n

        em.w("tensor", sV, prev_rope_v)
        em.w("tensor", sV, prev_vt_v)
        o_par = 0
        o_free = {0: 0, 1: 0}
        den_free = 0
        rb_free = 0
        for qt in range(8):
            q0 = qt * 512
            for hh in range(H):
                nkb = 4 * qt + 4
                ops_ = ps[2 + o_par]
                exp_mark = {}
                ready = {}
                for kb in range(nkb + 1):
                    if kb < nkb:
                        if kb >= 2:
                            em.w("tensor", sA, exp_mark[kb - 2])
                        em.op("tensor", lambda t, k=kb, h_=hh, qq=q0, b_=b: t.matmul(
                            ps[k % 2][:, :],
                            k_sb[:, k * 128:(k + 1) * 128],
                            q_sb[:, h_ * 4096 + qq: h_ * 4096 + qq + 512],
                            start=True, stop=True), sT)
                        em.w("scalar", sT)
                        em.op("scalar", lambda s, k=kb: s.activation(
                            out=p_bf[:, (k % 2) * 512:(k % 2) * 512 + 512],
                            in_=ps[k % 2][:, :], func=AF.Exp,
                            bias=nshift_c[:, 0:1], scale=sc_c[:, 0:1]), sA)
                        exp_mark[kb] = sA.n
                        if kb >= 4 * qt:
                            em.w("vector", sA, exp_mark[kb])
                            em.op("vector", lambda v, k=kb, j=kb - 4 * qt:
                                  v.tensor_tensor(
                                      p_bf[:, (k % 2) * 512:(k % 2) * 512 + 512],
                                      p_bf[:, (k % 2) * 512:(k % 2) * 512 + 512],
                                      masks_bf[:, j * 512:(j + 1) * 512],
                                      ALU.mult), sV)
                            ready[kb] = ("v", sV.n)
                        else:
                            ready[kb] = ("a", exp_mark[kb])
                    if kb >= 1:
                        k2 = kb - 1
                        kind, val = ready[k2]
                        em.w("tensor", sA if kind == "a" else sV, val)
                        if k2 == 0:
                            em.w("tensor", sV, o_free[o_par])
                            em.w("tensor", sV, den_free)
                        em.op("tensor", lambda t, k=k2, o_=ops_,
                              st=(k2 == 0), sp=(k2 == nkb - 1): t.matmul(
                                  o_[:, :], v_sb[:, k * 128:(k + 1) * 128],
                                  p_bf[:, (k % 2) * 512:(k % 2) * 512 + 512],
                                  start=st, stop=sp), None)
                        em.op("tensor", lambda t, k=k2,
                              st=(k2 == 0), sp=(k2 == nkb - 1): t.matmul(
                                  ps[4][0:1, :], ones_col_bf[:],
                                  p_bf[:, (k % 2) * 512:(k % 2) * 512 + 512],
                                  start=st, stop=sp), sT)
                oden_done = sT.n
                em.w("vector", sT, oden_done)
                em.op("vector", lambda v: v.reciprocal(rden[:], ps[4][0:1, :]), sV)
                den_free = sV.n
                em.w("tensor", sV, den_free)
                em.w("tensor", sV, rb_free)
                em.op("tensor", lambda t: t.matmul(ps[5][:, :], ones_row_f[:],
                                                   rden[:], start=True,
                                                   stop=True), sT)
                em.w("vector", sT)

                def v_norm(v, o_=ops_):
                    v.tensor_copy(rb_sb[:], ps[5][:, :])
                    return v.tensor_tensor(obf[:], o_[:, :], rb_sb[:], ALU.mult)

                em.op("vector", v_norm, sV)
                rb_free = sV.n
                o_free[o_par] = sV.n
                em.w("sync", sV)
                em.op("sync", lambda e, h_=hh, col=q0, o_=o_inb: e.dma_start(
                    out=o_[h_ * 128:(h_ + 1) * 128, col:col + 512],
                    in_=obf[:]), sL, 16)
                o_wr_last = sL.n
                em.w("vector", sL)
                o_par ^= 1

        o_allb = o_all0 if b == 0 else o_all1
        em.w("gpsimd", sL, o_wr_last)
        em.op("gpsimd", lambda g, i_=o_inb, o_=o_allb: g.collective_compute(
            "AllGather", ALU.bypass, replica_groups=[list(range(8))],
            ins=[i_.ap().opt()], outs=[o_.ap().opt()]), sG)
        if b == 0:
            ago0_done = sG.n
        else:
            ago1_done = sG.n

    def load_o_half(e, src, coff):
        cid = nc.partition_id(engines=[mybir.EngineType.SP])
        return e.dma_start(
            out=o_sb[:, coff:coff + CD * 512].rearrange("p (c t) -> p c t",
                                                        c=CD),
            in_=src[:, bass.ds(cid * 512, 512)]
            .rearrange("(c p) t -> p c t", p=128))

    em.w("sync", sG, ago0_done)
    em.op("sync", lambda e: load_o_half(e, o_all0, 0), sL, 16)
    em.w("sync", sG, ago1_done)
    em.op("sync", lambda e: load_o_half(e, o_all1, CD * 512), sL, 16)
    o_sb_ld = sL.n

    x2_written = {}
    x2_ready_v = {}
    sq_mark = {}
    ph4_first = True
    for mo in range(8):
        em.w("tensor", sG, wo_ld[mo])
        if ph4_first:
            em.w("tensor", sL, o_sb_ld)
            em.w("tensor", sV)
            ph4_first = False
        for tb in range(8):
            idx = mo * 8 + tb
            if idx >= 2:
                em.w("sync", sV, x2_ready_v[idx - 2])
            em.op("sync", lambda e, m=mo, i=tb, x=(mo * 8 + tb) % 2: e.dma_start(
                out=xres[:, x * 512:x * 512 + 512],
                in_=x_my[i * 128:(i + 1) * 128, m * 512:(m + 1) * 512]), sL, 16)
            xres_ld = sL.n
            if idx >= 2:
                em.w("tensor", sV, x2_ready_v[idx - 2])

            def womm(t, m=mo, i=tb):
                r = None
                ob = (i // 4) * CD * 512
                ii = i % 4
                for cdh in range(CD):
                    r = t.matmul(
                        ps[i % 2][:, :],
                        o_sb[:, ob + cdh * 512 + ii * 128:
                              ob + cdh * 512 + (ii + 1) * 128],
                        wo_sb[:, (m % 2) * CD * 512 + cdh * 512:
                              (m % 2) * CD * 512 + (cdh + 1) * 512],
                        start=(cdh == 0), stop=(cdh == CD - 1))
                return r

            em.op("tensor", womm, sT)
            wo_cons = sT.n
            em.w("vector", sT, wo_cons)
            em.w("vector", sL, xres_ld)
            if idx >= 2:
                em.w("vector", sL, x2_written[idx - 2])
                em.w("vector", sA, sq_mark[idx - 2])
            em.op("vector", lambda v, i=tb, x=idx % 2: v.tensor_tensor(
                x2st[:, x * 512:x * 512 + 512], ps[i % 2][:, :],
                xres[:, x * 512:x * 512 + 512], ALU.add), sV)
            x2_ready_v[idx] = sV.n
            em.op("vector", lambda v, c=tb * 8 + mo, x=idx % 2: v.reduce_sum(
                smP[:, c:c + 1], x2st[:, x * 512:x * 512 + 512], AX), sV)
            em.w("scalar", sV, x2_ready_v[idx])
            em.op("scalar", lambda s, c=tb * 8 + mo, x=idx % 2: s.activation(
                out=sq2s[:], in_=x2st[:, x * 512:x * 512 + 512],
                func=AF.Square, accum_out=sqP[:, c:c + 1]), sA)
            sq_mark[idx] = sA.n
            em.w("sync", sV, x2_ready_v[idx])
            em.op("sync", lambda e, m=mo, i=tb, x=idx % 2: e.dma_start(
                out=x2_scr[i * 128:(i + 1) * 128, m * 512:(m + 1) * 512],
                in_=x2st[:, x * 512:x * 512 + 512]), sL, 16)
            x2_written[idx] = sL.n
        if mo + 2 < 8:
            em.w("gpsimd", sT, wo_cons)
            em.op("gpsimd", lambda g, m_=mo + 2: g.dma_start(
                out=wo_sb[:, (m_ % 2) * CD * 512:(m_ % 2) * CD * 512 + CD * 512]
                .rearrange("p (c j) -> p c j", c=CD),
                in_=wo_d[:, m_ * 512:(m_ + 1) * 512]
                .rearrange("(c p) j -> p c j", p=128)), sG, 16)
            wo_ld[mo + 2] = sG.n

    em.w("vector", sA, sq_mark[63])
    em.w("vector", sV)

    def v_red2(v):
        r = None
        for tb in range(8):
            v.reduce_sum(sq2a[:, tb:tb + 1], sqP[:, tb * 8:(tb + 1) * 8], AX)
        for tb in range(8):
            r = v.reduce_sum(sum2[:, tb:tb + 1], smP[:, tb * 8:(tb + 1) * 8], AX)
        return r

    em.op("vector", v_red2, sB)
    em.w("vector", sB)
    em.op("vector", lambda v: v.tensor_scalar(negm1[:], sum2[:], -1.0 / D,
                                              None, ALU.mult), sB)
    em.w("vector", sB)
    em.op("vector", lambda v: v.tensor_scalar(var1[:], sq2a[:], 1.0 / D,
                                              None, ALU.mult), sB)
    em.w("vector", sB)
    em.op("vector", lambda v: v.tensor_tensor(tmp1[:], negm1[:], negm1[:],
                                              ALU.mult), sB)
    em.w("vector", sB)
    em.op("vector", lambda v: v.tensor_tensor(var1[:], var1[:], tmp1[:],
                                              ALU.subtract), sV)
    em.w("scalar", sV)

    def s_stats2(s):
        s.activation(out=tmp1[:], in_=var1[:], func=AF.Ln, bias=eps_c[:, 0:1])
        return s.activation(out=rstd1[:], in_=tmp1[:], func=AF.Exp,
                            scale=nh_c[:, 0:1])

    em.op("scalar", s_stats2, sA)
    ln2_stats = sA.n

    xl2_wr = {}
    xl2_ap = {}
    for tb in range(8):
        em.w("sync", sL, x2_written[56 + tb])
        if tb >= 2:
            em.w("sync", sV, xl2_ap[tb - 2])
        em.op("sync", lambda e, i=tb: e.dma_start(
            out=x2rd[:, (i % 2) * 4096:(i % 2) * 4096 + 4096],
            in_=x2_scr[i * 128:(i + 1) * 128, :]), sL, 16)
        em.w("vector", sL)
        em.w("vector", sA, ln2_stats)
        if tb >= 2:
            em.w("vector", sL, xl2_wr[tb - 2])
        em.op("vector", lambda v, i=tb: v.tensor_scalar(
            xl2w[:, (i % 2) * 4096:(i % 2) * 4096 + 4096],
            x2rd[:, (i % 2) * 4096:(i % 2) * 4096 + 4096],
            negm1[:, i:i + 1], rstd1[:, i:i + 1], ALU.add, ALU.mult), sV)
        xl2_ap[tb] = sV.n
        em.w("sync", sV)
        em.op("sync", lambda e, i=tb: e.dma_start(
            out=xl2_scr[i * 128:(i + 1) * 128, :],
            in_=xl2w[:, (i % 2) * 4096:(i % 2) * 4096 + 4096]), sL, 16)
        xl2_wr[tb] = sL.n
    em.w("sync", sL, xl2_wr[7])
    for cd in range(CD):
        em.op("sync", lambda e, d=cd: e.dma_start(
            out=xl2t[:, d * 1024:(d + 1) * 1024],
            in_=xl2_scr[:, d * 128:(d + 1) * 128], transpose=True), sL, 16)
    xl2t_ld = sL.n

    em.w("tensor", sL, xl2t_ld)
    gt_done = 0
    for tth in range(2):
        if tth == 1:
            em.w("tensor", sV, gt_done)

        def gmm(t, th=tth):
            r = None
            for cd in range(CD):
                r = t.matmul(ps[2][0:9, :], g9_sb[:, cd * 9:(cd + 1) * 9],
                             xl2t[:, cd * 1024 + th * 512: cd * 1024 + th * 512 + 512],
                             start=(cd == 0), stop=(cd == CD - 1))
            return r

        em.op("tensor", gmm, sT)
        em.w("vector", sT)
        em.op("vector", lambda v, th=tth: v.tensor_copy(
            gt_sb[:, th * 512:(th + 1) * 512], ps[2][0:9, :]), sV)
        gt_done = sV.n
    em.w("tensor", sV, gt_done)

    def gtr(t):
        r = None
        for i in range(8):
            r = t.matmul(ps[3][:, i * 9:(i + 1) * 9],
                         gt_sb[:, i * 128:(i + 1) * 128], i9_f[:, :],
                         start=True, stop=True)
        return r

    em.op("tensor", gtr, sT)
    gtr_done = sT.n

    wt_copied = {}
    gate_read = 0
    for tb in range(8):
        em.w("vector", sT, gtr_done)
        if tb >= 1:
            em.w("vector", sT)
        em.op("vector", lambda v, i=tb: v.reduce_max(
            gmax[:], ps[3][:, i * 9:i * 9 + 8], AX, negate=True), sV)
        em.w("scalar", sV)
        em.w("scalar", sB, gate_read)
        em.op("scalar", lambda s, i=tb: s.activation(
            out=gate_e[:], in_=ps[3][:, i * 9:i * 9 + 8], func=AF.Exp,
            bias=gmax[:, 0:1]), sA)
        em.op("scalar", lambda s, i=tb: s.activation(
            out=esig[:], in_=ps[3][:, i * 9 + 8:i * 9 + 9], func=AF.Exp,
            scale=none_c[:, 0:1], bias=nbthr_c[:, 0:1]), sA)
        em.w("vector", sA)

        def vg2(v):
            v.reduce_sum(gsum[:], gate_e[:], AX)
            return v.tensor_scalar(esig[:], esig[:], 1.0, None, ALU.add)

        em.op("vector", vg2, sB)
        em.w("vector", sB)

        def vg3(v):
            v.reciprocal(grec[:], gsum[:])
            return v.reciprocal(sig[:], esig[:])

        em.op("vector", vg3, sB)
        em.w("vector", sB)

        em.op("vector", lambda v: v.tensor_scalar(thr_c[:], sig[:], 0.125,
                                                  None, ALU.mult), sB)
        em.w("vector", sB)
        em.op("vector", lambda v: v.tensor_scalar(
            w8[:], gate_e[:], grec[:, 0:1], thr_c[:, 0:1],
            ALU.mult, ALU.subtract), sB)
        gate_read = sB.n
        em.w("vector", sB)
        em.op("vector", lambda v: v.tensor_scalar_max(w8[:], w8[:], 0.0), sB)
        em.w("vector", sB)
        em.op("vector", lambda v: v.reduce_sum(wsum[:], w8[:], AX), sB)
        em.w("vector", sB)
        em.op("vector", lambda v: v.tensor_single_scalar(wiz[:], wsum[:], 0.0,
                                                         ALU.is_equal), sB)
        em.w("vector", sB)
        em.op("vector", lambda v: v.tensor_tensor(wsum[:], wsum[:], wiz[:],
                                                  ALU.add), sB)
        em.w("vector", sB)
        em.op("vector", lambda v: v.reciprocal(wrec[:], wsum[:]), sB)
        em.w("vector", sB)
        em.op("vector", lambda v: v.tensor_scalar(w_bf[:], w8[:], wrec[:, 0:1],
                                                  None, ALU.mult), sV)
        em.w("tensor", sV)
        if tb >= 1:
            em.w("tensor", sV, wt_copied[tb - 1])
        em.op("tensor", lambda t: t.matmul(ps[4][0:8, 0:128], w_bf[:],
                                           ident_bf[:], start=True, stop=True), sT)
        em.w("vector", sT)
        em.op("vector", lambda v, i=tb: v.tensor_copy(
            wt_sb[:, i * 128:(i + 1) * 128], ps[4][0:8, 0:128]), sV)
        wt_copied[tb] = sV.n

    wexp_done = 0
    for tth in range(2):
        em.w("tensor", sV, wt_copied[7])
        if tth == 1:
            em.w("tensor", sV, wexp_done)
        em.op("tensor", lambda t, th=tth: t.matmul(
            ps[5][:, :], e2_bf[:], wt_sb[:, th * 512:(th + 1) * 512],
            start=True, stop=True), sT)
        em.w("vector", sT)
        em.op("vector", lambda v, th=tth: v.tensor_copy(
            wexp_sb[:, th * 512:(th + 1) * 512], ps[5][:, :]), sV)
        wexp_done = sV.n

    hp_done = 0
    for tth in range(2):
        em.w("tensor", sV, wexp_done)
        if tth == 1:
            em.w("tensor", sV, hp_done)

        def hmm(t, th=tth):
            r = None
            for cd in range(CD):
                r = t.matmul(ps[6][:, :], at_sb[:, cd * 128:(cd + 1) * 128],
                             xl2t[:, cd * 1024 + th * 512: cd * 1024 + th * 512 + 512],
                             start=(cd == 0), stop=(cd == CD - 1))
            return r

        em.op("tensor", hmm, sT)
        em.w("vector", sT)
        em.op("vector", lambda v, th=tth: v.tensor_tensor(
            hp_bf[:, th * 512:(th + 1) * 512], ps[6][:, :],
            wexp_sb[:, th * 512:(th + 1) * 512], ALU.mult), sV)
        hp_done = sV.n

    wb_ld = {}
    for m in range(2):
        em.w("gpsimd", sL, xl2t_ld)
        em.op("gpsimd", lambda g, m_=m: g.dma_start(
            out=wb_sb[:, (m_ % 2) * CD * 512:(m_ % 2) * CD * 512 + CD * 512]
            .rearrange("p (c j) -> p c j", c=CD),
            in_=wbT_d[:, m_ * 512:(m_ + 1) * 512]
            .rearrange("(c p) j -> p c j", p=128)), sG, 16)
        wb_ld[m] = sG.n

    out_v = {}
    out_wr = {}
    st_last = 0
    ph6_first = True
    for mo in range(8):
        em.w("tensor", sG, wb_ld[mo])
        if ph6_first:
            em.w("tensor", sV, hp_done)
            ph6_first = False
        for tb in range(8):
            idx = mo * 8 + tb
            if idx < 2:
                em.w("sync", sV, xl2_ap[7])
            if idx >= 2:
                em.w("sync", sV, out_v[idx - 2])
            em.op("sync", lambda e, m=mo, i=tb, x=(mo * 8 + tb) % 2: e.dma_start(
                out=x2c[:, x * 512:x * 512 + 512],
                in_=x2_scr[i * 128:(i + 1) * 128, m * 512:(m + 1) * 512]), sL, 16)
            x2c_ld = sL.n
            if idx >= 2:
                em.w("tensor", sV, out_v[idx - 2])

            def bmm(t, m=mo, i=tb):
                for cd in range(CD):
                    t.matmul(ps[i % 2][:, :],
                             xl2t[:, cd * 1024 + i * 128: cd * 1024 + (i + 1) * 128],
                             wb_sb[:, (m % 2) * CD * 512 + cd * 512:
                                   (m % 2) * CD * 512 + (cd + 1) * 512],
                             start=(cd == 0), stop=False)
                t.matmul(ps[i % 2][:, :], hp_bf[:, i * 128:(i + 1) * 128],
                         bc_sb[:, m * 512:(m + 1) * 512], start=False, stop=False)
                return t.matmul(ps[i % 2][:, :], ones_row_bf[:],
                                bb_row_bf[0:1, m * 512:(m + 1) * 512],
                                start=False, stop=True)

            em.op("tensor", bmm, sT)
            wb_cons = sT.n
            em.w("vector", sT, wb_cons)
            em.w("vector", sL, x2c_ld)
            if idx >= 2:
                em.w("vector", sL, out_wr[idx - 2])
            em.op("vector", lambda v, i=tb, x=idx % 2: v.tensor_tensor(
                outst[:, x * 512:x * 512 + 512], ps[i % 2][:, :],
                x2c[:, x * 512:x * 512 + 512], ALU.add), sV)
            out_v[idx] = sV.n
            em.w("sync", sV, out_v[idx])
            em.op("sync", lambda e, m=mo, i=tb, x=idx % 2: e.dma_start(
                out=out_d[i * 128:(i + 1) * 128, m * 512:(m + 1) * 512],
                in_=outst[:, x * 512:x * 512 + 512]), sL, 16)
            out_wr[idx] = sL.n
            st_last = sL.n
        if mo + 2 < 8:
            em.w("gpsimd", sT, wb_cons)
            em.op("gpsimd", lambda g, m_=mo + 2: g.dma_start(
                out=wb_sb[:, (m_ % 2) * CD * 512:(m_ % 2) * CD * 512 + CD * 512]
                .rearrange("p (c j) -> p c j", c=CD),
                in_=wbT_d[:, m_ * 512:(m_ + 1) * 512]
                .rearrange("(c p) j -> p c j", p=128)), sG, 16)
            wb_ld[mo + 2] = sG.n

    em.w("sync", sL, st_last)

    with nc.Block() as block:
        @block.sync
        def _(e):
            for f in em.q["sync"]:
                f(e)

        @block.gpsimd
        def _(e):
            for f in em.q["gpsimd"]:
                f(e)

        @block.scalar
        def _(e):
            for f in em.q["scalar"]:
                f(e)

        @block.vector
        def _(e):
            for f in em.q["vector"]:
                f(e)

        @block.tensor
        def _(e):
            for f in em.q["tensor"]:
                f(e)

    return nc


def kernel(L0_attn_X, L0_attn_Wq, L0_attn_Wk, L0_attn_Wv, L0_attn_Wo,
           L0_ffn_W_base, L0_ffn_b_base, L0_ffn_W_gate, L0_ffn_W_thr,
           L0_ffn_b_thr, L0_ffn_A, L0_ffn_B):
    X2 = np.ascontiguousarray(np.asarray(L0_attn_X, dtype=np.float32).reshape(NT, D))
    Wq = np.asarray(L0_attn_Wq, dtype=np.float32)
    Wk = np.asarray(L0_attn_Wk, dtype=np.float32)
    Wv = np.asarray(L0_attn_Wv, dtype=np.float32)
    Wo = np.ascontiguousarray(np.asarray(L0_attn_Wo, dtype=np.float32))
    WbT = np.ascontiguousarray(np.asarray(L0_ffn_W_base, dtype=np.float32).T)
    A_flat = np.asarray(L0_ffn_A, dtype=np.float32).reshape(128, D)
    a_tile = np.ascontiguousarray(A_flat.reshape(128, CD, 128).transpose(2, 1, 0))
    Bcat = np.ascontiguousarray(
        np.asarray(L0_ffn_B, dtype=np.float32).transpose(0, 2, 1).reshape(128, D))
    g9 = np.ascontiguousarray(
        np.concatenate([np.asarray(L0_ffn_W_gate, dtype=np.float32),
                        np.asarray(L0_ffn_W_thr, dtype=np.float32)], 0)
        .reshape(9, CD, 128).transpose(2, 1, 0))
    bb = np.ascontiguousarray(np.asarray(L0_ffn_b_base, dtype=np.float32)
                              .reshape(1, D))
    bthr = np.ascontiguousarray(np.asarray(L0_ffn_b_thr, dtype=np.float32)
                                .reshape(1, 1))

    in_maps = []
    for c in range(8):
        wq_c = np.ascontiguousarray(
            Wq[:, c * 512:(c + 1) * 512].reshape(CD, 128, 512).transpose(1, 0, 2))
        wk_c = np.ascontiguousarray(
            Wk[:, c * 128:(c + 1) * 128].reshape(CD, 128, 128).transpose(1, 0, 2))
        wv_c = np.ascontiguousarray(
            Wv[:, c * 128:(c + 1) * 128].reshape(CD, 128, 128).transpose(1, 0, 2))
        x_own = np.concatenate(
            [X2[c * 512:(c + 1) * 512],
             X2[4096 + c * 512:4096 + (c + 1) * 512]], axis=0)
        in_maps.append({
            "x_my": np.ascontiguousarray(x_own),
            "wq_t": wq_c, "wk_t": wk_c, "wv_t": wv_c,
            "wo_d": Wo, "wbT_d": WbT, "a_t": a_tile, "bcat_d": Bcat,
            "g9_d": g9, "bb_d": bb, "bthr_d": bthr,
        })

    nc = build_nc()
    res = bass_utils.run_bass_kernel_spmd(nc, in_maps, core_ids=list(range(8)))
    out = np.empty((NT, D), dtype=np.float32)
    for c in range(8):
        oc = res.results[c]["out_d"]
        out[c * 512:(c + 1) * 512] = oc[0:512]
        out[4096 + c * 512:4096 + (c + 1) * 512] = oc[512:1024]
    return out.reshape(2, 4096, 4096)
